# version 08
# baseline (speedup 1.0000x reference)
import contextlib
import numpy as np
import ml_dtypes
import concourse.bass as bass
import concourse.mybir as mybir
from concourse.bass_utils import run_bass_kernel_spmd

F32 = mybir.dt.float32
BF16 = mybir.dt.bfloat16
AF = mybir.ActivationFunctionType
ALU = mybir.AluOpType
AXX = mybir.AxisListType.X

D = 2048
S = 2048
H = 16
HD = 128
HID = 8192
NE = 2
NDC = D // 128
TB = 512
NTB = S // TB
TH = 1024
NTH = S // TH
HB = 512
NHB = HID // HB
WPC = 256
SCALE = 1.0 / float(np.sqrt(HD))
EPS = 1e-5
NEG = -1e9


def build_program():
    nc = bass.Bass("TRN2", target_bir_lowering=False)

    xT = nc.dram_tensor("xT", [D, S], F32, kind="ExternalInput").ap()
    xtm = nc.dram_tensor("xtm", [S, D], F32, kind="ExternalInput").ap()
    wqkvT = nc.dram_tensor("wqkvT", [D, 3 * D], F32, kind="ExternalInput").ap()
    bqkv = nc.dram_tensor("bqkv", [1, 3 * D], F32, kind="ExternalInput").ap()
    woT = nc.dram_tensor("woT", [D, D], F32, kind="ExternalInput").ap()
    bout = nc.dram_tensor("bout", [1, D], F32, kind="ExternalInput").ap()
    wrT = nc.dram_tensor("wrT", [D, 2], F32, kind="ExternalInput").ap()
    wfcT = nc.dram_tensor("wfcT", [NE, D, HID], F32, kind="ExternalInput").ap()
    wpT = nc.dram_tensor("wpT", [NE, HID, D], F32, kind="ExternalInput").ap()
    identd = nc.dram_tensor("identd", [128, 128], F32, kind="ExternalInput").ap()
    trid = nc.dram_tensor("trid", [128, 128], F32, kind="ExternalInput").ap()
    ali2d = nc.dram_tensor("ali2d", [2, H * S], F32, kind="ExternalInput").ap()

    xn_sp = nc.dram_tensor("xn_sp", [D, S], BF16).ap()
    q_sp = nc.dram_tensor("q_sp", [D, S], BF16).ap()
    k_sp = nc.dram_tensor("k_sp", [D, S], BF16).ap()
    v_sp = nc.dram_tensor("v_sp", [S, D], BF16).ap()
    s_sp = nc.dram_tensor("s_sp", [S, D], BF16).ap()

    out = nc.dram_tensor("out", [S, D], F32, kind="ExternalOutput").ap()

    es = contextlib.ExitStack()

    def sb(name, shape, dt):
        return es.enter_context(nc.sbuf_tensor(name, shape, dt))

    def pst(name, shape):
        return es.enter_context(nc.psum_tensor(name, shape, F32))

    psA = pst("psA", [128, 2048])
    psB = pst("psB", [128, 512])
    psC = pst("psC", [128, 512])
    psD = pst("psD", [128, 512])
    psE = pst("psE", [128, 512])

    A1 = sb("A1", [128, 16384], F32); A1b = A1.bitcast(BF16)
    A2 = sb("A2", [128, 12288], F32); A2b = A2.bitcast(BF16)
    A3 = sb("A3", [128, 4096], F32); A3b = A3.bitcast(BF16)
    A4 = sb("A4", [128, 6144], F32); A4b = A4.bitcast(BF16)
    A5 = sb("A5", [128, 2048], F32); A5b = A5.bitcast(BF16)
    XN4 = sb("XN4", [128, 16384], BF16)

    ident = sb("ident", [128, 128], BF16)
    identf = sb("identf", [128, 128], F32)
    tri = sb("tri", [128, 128], F32)
    ones2 = sb("ones2", [2, 512], BF16)
    onesf1 = sb("onesf1", [1, 128], F32)
    d_row = sb("d_row", [1, S], F32)
    wr_hi = sb("wr_hi", [128, NDC], BF16)
    wr_lo = sb("wr_lo", [128, NDC], BF16)
    negmax = sb("negmax", [128, 2], F32)
    rowsum = sb("rowsum", [128, 2], F32)
    smt = sb("smt", [128, 8], F32)
    rcs = sb("rcs", [128, 2], F32)
    wrs = sb("wrs", [128, 4], F32)

    bias_f = A1.ap()[0:1, 0:6144]
    bias_lo = A1.ap()[0:1, 6144:12288]
    bias_hib = A1b.ap()[0:1, 24576:30720]
    bias_lob = A1b.ap()[0:1, 30720:32768]
    wr_f = A5.ap()[:, 64:96]
    bq2 = A3b.ap()[0:2, 0:6144]
    bo2 = A3b.ap()[0:2, 6144:8192]
    xtm_buf = [A2.ap()[:, 0:2048], A2.ap()[:, 2048:4096]]
    xsq_buf = A2.ap()[:, 4096:6144]
    xf_buf = A2.ap()[:, 6144:8192]
    Rb = A4.ap()[:, 0:2048]
    negMRb = A4.ap()[:, 2048:4096]
    nmr_row = A4.ap()[0:1, 4096:6144]
    r_row = A2.ap()[0:1, 8192:10240]
    m_st = A5.ap()[:, 0:16]
    r_st = A5.ap()[:, 16:32]
    nmr_st = A5.ap()[:, 32:48]
    ct = A5.ap()[:, 48:64]
    xn_sb = A1b.ap()[:, 0:32768]
    xn_lob = A2b.ap()[:, 16384:18432]

    wqk_buf = [A2b.ap()[:, 16384:18432], A2b.ap()[:, 18432:20480]]
    vfm_st = A2b.ap()[:, 20480:22528]
    vtm_st = A2b.ap()[:, 22528:24576]

    q_tb = A1b.ap()[:, 0:8192]
    ctx_tb = A1b.ap()[:, 8192:16384]
    wo_buf = [A1b.ap()[:, 16384:18432], A1b.ap()[:, 18432:20480]]
    k_buf = [A2b.ap()[:, 0:2048], A2b.ap()[:, 2048:4096]]
    v_buf = [A2b.ap()[:, 4096:8192], A2b.ap()[:, 8192:12288]]
    p_ib = [A2b.ap()[:, 12288:14336], A2b.ap()[:, 14336:16384]]
    pT_ib = [A2b.ap()[:, 16384:18432], A2b.ap()[:, 18432:20480]]
    s_st = [A4b.ap()[:, 0:2048], A4b.ap()[:, 2048:4096]]
    ali_buf = [A4b.ap()[0:2, 4096:6144], A4b.ap()[0:2, 6144:8192]]

    acc = A1.ap()[:, 0:16384]
    wfc_buf = [A2b.ap()[:, 0:4096], A2b.ap()[:, 4096:8192]]
    wp_buf = [A2b.ap()[:, 8192:16384], A2b.ap()[:, 16384:24576]]
    a_buf = [A3b.ap()[:, 0:4096], A3b.ap()[:, 4096:8192]]
    wv_t = [[A4.ap()[:, p * 2048 + tch * 512:p * 2048 + (tch + 1) * 512]
             for tch in range(2)] for p in range(2)]
    sqs_t = [[A4.ap()[:, p * 2048 + 1024 + tch * 512:p * 2048 + 1024 + (tch + 1) * 512]
              for tch in range(2)] for p in range(2)]
    g_bc = [A4.ap()[:, 4096:5120], A4.ap()[:, 5120:6144]]
    x_cb = A4.ap()[:, 0:2048]
    o_cb = A4.ap()[:, 2048:4096]
    grow = [A5.ap()[0:1, 0:1024], A5.ap()[0:1, 1024:2048]]
    abs_row = A5.ap()[0:1, 0:1024]
    s_cb = A5b.ap()[:, 0:2048]

    block = es.enter_context(nc.Block())

    sem = {}
    for name in ["lw", "lx", "mm", "vd", "sd", "gd", "st", "ph"]:
        sem[name] = es.enter_context(nc.semaphore(name))

    Cn = {n: 0 for n in sem}

    def inc(inst, s, n=1):
        inst.then_inc(sem[s], n)
        Cn[s] += n
        return Cn[s]

    def W(e, s, val=None):
        e.wait_ge(sem[s], Cn[s] if val is None else val)

    T, V, Sc, G, Sy = block.tensor, block.vector, block.scalar, block.gpsimd, block.sync

    def barrier():
        def _fl_y(e):
            W(e, "st"); W(e, "lx")
        Sy(_fl_y)
        def _fl_g(e):
            W(e, "lw")
        G(_fl_g)
        for emit in (T, V, Sc, G, Sy):
            def _f(e):
                inc(e.nop(), "ph", 1)
            emit(_f)
        tgt = Cn["ph"]
        for emit in (T, V, Sc, G, Sy):
            def _g(e, tgt=tgt):
                e.wait_ge(sem["ph"], tgt)
            emit(_g)

    with es:
        def p0g(g):
            inc(g.dma_start(ident[:, :], identd), "lw", 16)
            inc(g.dma_start(identf[:, :], identd), "lw", 16)
            inc(g.dma_start(tri[:, :], trid), "lw", 16)
            inc(g.dma_start(bias_f, bqkv), "lw", 16)
            inc(g.dma_start(bias_hib[:, 0:6144], bqkv), "lw", 16)
            inc(g.dma_start(wr_f.rearrange("p (dc e) -> p dc e", e=2),
                            wrT.rearrange("(dc p) e -> p dc e", p=128)), "lw", 16)
        G(p0g)

        def p0v(v):
            W(v, "lw")
            v.memset(ones2[:, :], 1.0)
            v.memset(onesf1[:, :], 1.0)
            v.tensor_tensor(bias_lo, bias_f, bias_hib[:, 0:6144], ALU.subtract)
            inc(v.tensor_copy(bias_lob[:, 0:2048], bias_lo[:, 0:2048]), "vd")
            inc(v.tensor_copy(A2b.ap()[0:1, 0:4096], bias_lo[:, 2048:6144]), "vd")
            wrd_f = A5.ap()[:, 96:112]
            wrl_f = A5.ap()[:, 112:128]
            for dc in range(NDC):
                v.tensor_tensor(wrd_f[:, dc:dc + 1], wr_f[:, 2 * dc:2 * dc + 1],
                                wr_f[:, 2 * dc + 1:2 * dc + 2], ALU.subtract)
            inc(v.nop(), "vd"); W(v, "vd")
            inc(v.tensor_copy(wr_hi[:, :], wrd_f), "vd"); W(v, "vd")
            inc(v.tensor_tensor(wrl_f, wrd_f, wr_hi[:, :], ALU.subtract), "vd"); W(v, "vd")
            inc(v.tensor_copy(wr_lo[:, :], wrl_f), "vd")
        V(p0v)

        def p0g2(g):
            W(g, "vd")
            inc(g.dma_start(bq2[0:1, :], bias_hib[:, 0:6144]), "lw", 16)
            inc(g.dma_start(bq2[1:2, 0:2048], bias_lob[:, 0:2048]), "lw", 16)
            inc(g.dma_start(bq2[1:2, 2048:6144], A2b.ap()[0:1, 0:4096]), "lw", 16)
            inc(g.dma_start(bias_f[:, 0:2048], bout), "lw", 16)
            inc(g.dma_start(bias_hib[:, 0:2048], bout), "lw", 16)
        G(p0g2)

        def p0v2(v):
            W(v, "lw")
            v.tensor_tensor(bias_lo[:, 0:2048], bias_f[:, 0:2048],
                            bias_hib[:, 0:2048], ALU.subtract)
            inc(v.tensor_copy(bias_lob[:, 0:2048], bias_lo[:, 0:2048]), "vd")
        V(p0v2)

        def p0g3(g):
            W(g, "vd")
            inc(g.dma_start(bo2[0:1, :], bias_hib[:, 0:2048]), "lw", 16)
            inc(g.dma_start(bo2[1:2, :], bias_lob[:, 0:2048]), "lw", 16)
        G(p0g3)

        stat_rel = [None, None]
        for i in range(16):
            def p1l(y, i=i):
                if i == 0:
                    W(y, "lw")
                if stat_rel[i % 2] is not None:
                    W(y, "vd", stat_rel[i % 2])
                inc(y.dma_start(xtm_buf[i % 2], xtm[i * 128:(i + 1) * 128, :]), "lx", 16)
            Sy(p1l)

            def p1s(s, i=i):
                W(s, "lx")
                if i > 0:
                    W(s, "vd")
                inc(s.activation(xsq_buf, xtm_buf[i % 2], AF.Square), "sd")
            Sc(p1s)

            def p1v(v, i=i):
                W(v, "sd")
                v.reduce_sum(m_st[:, i:i + 1], xtm_buf[i % 2], axis=AXX)
                stat_rel[i % 2] = inc(v.reduce_sum(ct[:, 0:1], xsq_buf, axis=AXX), "vd")
                W(v, "vd")
                v.tensor_scalar(m_st[:, i:i + 1], m_st[:, i:i + 1], 1.0 / D, None, ALU.mult)
                v.tensor_scalar(ct[:, 0:1], ct[:, 0:1], 1.0 / D, None, ALU.mult)
                inc(v.tensor_tensor(ct[:, 1:2], m_st[:, i:i + 1], m_st[:, i:i + 1], ALU.mult), "vd")
                W(v, "vd")
                inc(v.tensor_tensor(ct[:, 2:3], ct[:, 0:1], ct[:, 1:2], ALU.subtract), "vd")
                W(v, "vd")
                inc(v.tensor_scalar(ct[:, 3:4], ct[:, 2:3], EPS, None, ALU.add), "vd")
            V(p1v)

            def p1r(s, i=i):
                W(s, "vd")
                inc(s.activation(ct[:, 4:5], ct[:, 3:4], AF.Ln), "sd")
                W(s, "sd")
                inc(s.activation(r_st[:, i:i + 1], ct[:, 4:5], AF.Exp, scale=-0.5), "sd")
            Sc(p1r)

            def p1m(v, i=i):
                W(v, "sd")
                inc(v.tensor_tensor(ct[:, 5:6], m_st[:, i:i + 1], r_st[:, i:i + 1], ALU.mult), "vd")
                W(v, "vd")
                inc(v.tensor_scalar(nmr_st[:, i:i + 1], ct[:, 5:6], -1.0, None, ALU.mult), "vd")
            V(p1m)

        for i in range(16):
            def p1t(t, i=i):
                W(t, "vd")
                t.matmul(psD[0:1, 0:128], nmr_st[:, i:i + 1], identf[:, :], start=True, stop=True)
                inc(t.matmul(psD[0:1, 128:256], r_st[:, i:i + 1], identf[:, :],
                             start=True, stop=True), "mm")
            T(p1t)

            def p1te(v, i=i):
                W(v, "mm")
                v.tensor_copy(nmr_row[:, i * 128:(i + 1) * 128], psD[0:1, 0:128])
                inc(v.tensor_copy(r_row[:, i * 128:(i + 1) * 128], psD[0:1, 128:256]), "vd")
            V(p1te)

        def p1b(t):
            W(t, "vd")
            pb = [psB, psC, psD, psE]
            for j in range(4):
                t.matmul(psA[:, j * 512:(j + 1) * 512], onesf1[:, :],
                         r_row[:, j * 512:(j + 1) * 512], start=True, stop=True)
            last = None
            for j in range(4):
                last = t.matmul(pb[j][:, :], onesf1[:, :],
                                nmr_row[:, j * 512:(j + 1) * 512], start=True, stop=True)
            inc(last, "mm")
        T(p1b)

        def p1be(v):
            W(v, "mm")
            pb = [psB, psC, psD, psE]
            for j in range(4):
                v.tensor_copy(Rb[:, j * 512:(j + 1) * 512], psA[:, j * 512:(j + 1) * 512])
            last = None
            for j in range(4):
                last = v.tensor_copy(negMRb[:, j * 512:(j + 1) * 512], pb[j][:, :])
            inc(last, "vd")
        V(p1be)

        xch_rel = [None, None]
        for dc in range(NDC):
            def p1xl(y, dc=dc):
                if dc == 0:
                    W(y, "vd")
                if xch_rel[dc % 2] is not None:
                    W(y, "gd", xch_rel[dc % 2])
                inc(y.dma_start(xtm_buf[dc % 2], xT[dc * 128:(dc + 1) * 128, :]), "lx", 16)
            Sy(p1xl)

            def p1xg(g, dc=dc):
                W(g, "lx")
                if dc > 0:
                    W(g, "vd")
                xch_rel[dc % 2] = inc(
                    g.tensor_tensor(xsq_buf, xtm_buf[dc % 2], Rb, ALU.mult), "gd")
            G(p1xg)

            def p1xv(v, dc=dc):
                W(v, "gd")
                v.tensor_tensor(xf_buf, xsq_buf, negMRb, ALU.add)
                inc(v.tensor_copy(xn_sb[:, dc * 2048:(dc + 1) * 2048], xf_buf), "vd")
                v.tensor_tensor(xsq_buf, xf_buf, xn_sb[:, dc * 2048:(dc + 1) * 2048], ALU.subtract)
                inc(v.tensor_copy(xn_lob, xsq_buf), "vd")
            V(p1xv)

            def p1xt(t, dc=dc):
                W(t, "vd")
                last = None
                for j in range(4):
                    xh = xn_sb[:, dc * 2048 + j * 512:dc * 2048 + (j + 1) * 512]
                    xl = xn_lob[:, j * 512:(j + 1) * 512]
                    t.matmul(psA[0:1, j * 512:(j + 1) * 512], wr_hi[:, dc:dc + 1], xh,
                             start=(dc == 0), stop=False)
                    t.matmul(psA[0:1, j * 512:(j + 1) * 512], wr_lo[:, dc:dc + 1], xh,
                             start=False, stop=False)
                    last = t.matmul(psA[0:1, j * 512:(j + 1) * 512], wr_hi[:, dc:dc + 1], xl,
                                    start=False, stop=(dc == NDC - 1))
                inc(last, "mm")
            T(p1xt)

        def p1de(v):
            W(v, "mm")
            inc(v.tensor_copy(d_row[:, :], psA[0:1, 0:2048]), "vd")
        V(p1de)

        def p1sp(y):
            W(y, "vd")
            inc(y.dma_start(xn_sp.rearrange("(dc p) t -> p dc t", p=128),
                            xn_sb.rearrange("p (dc t) -> p dc t", t=2048)), "st", 16)
        Sy(p1sp)

        barrier()

        pg = [[psA[:, j * 512:(j + 1) * 512] for j in range(4)],
              [psB[:, :], psC[:, :], psD[:, :], psE[:, :]]]
        wqk_rel = [None, None]
        bank_rel = [None, None]

        for which in range(2):
            col0 = which * D
            for oc in range(NDC):
                idx = which * NDC + oc

                def p2wl(g, col0=col0, oc=oc, idx=idx):
                    if wqk_rel[idx % 2] is not None:
                        W(g, "mm", wqk_rel[idx % 2])
                    inc(g.dma_start(
                        wqk_buf[idx % 2].rearrange("p (dc o) -> p dc o", o=128),
                        wqkvT.rearrange("(dc p) o -> p dc o", p=128)[:, :, col0 + oc * 128:col0 + (oc + 1) * 128]),
                        "lw", 16)
                G(p2wl)

                def p2mm(t, col0=col0, oc=oc, idx=idx):
                    W(t, "lw")
                    if bank_rel[idx % 2] is not None:
                        W(t, "vd", bank_rel[idx % 2])
                    grp = pg[idx % 2]
                    for dc in range(NDC):
                        for j in range(4):
                            t.matmul(grp[j], wqk_buf[idx % 2][:, dc * 128:(dc + 1) * 128],
                                     xn_sb[:, dc * 2048 + j * 512:dc * 2048 + (j + 1) * 512],
                                     start=(dc == 0), stop=False)
                    last = None
                    for j in range(4):
                        last = t.matmul(grp[j], bq2[:, col0 + oc * 128:col0 + (oc + 1) * 128],
                                        ones2[:, :], start=False, stop=True)
                    wqk_rel[idx % 2] = inc(last, "mm")
                T(p2mm)

                def p2ev(v, idx=idx, which=which):
                    W(v, "mm")
                    W(v, "st")
                    grp = pg[idx % 2]
                    last = None
                    for j in range(4):
                        dst = vfm_st[:, j * 512:(j + 1) * 512]
                        if which == 0:
                            last = v.tensor_scalar(dst, grp[j], SCALE, None, ALU.mult)
                        else:
                            last = v.tensor_copy(dst, grp[j])
                    bank_rel[idx % 2] = inc(last, "vd")
                V(p2ev)

                def p2st(y, oc=oc, which=which):
                    W(y, "vd")
                    spill = q_sp if which == 0 else k_sp
                    inc(y.dma_start(spill[oc * 128:(oc + 1) * 128, :], vfm_st), "st", 16)
                Sy(p2st)

        for oc in range(NDC):
            idx = oc

            def p2vwl(g, oc=oc, idx=idx):
                if wqk_rel[idx % 2] is not None:
                    W(g, "mm", wqk_rel[idx % 2])
                inc(g.dma_start(
                    wqk_buf[idx % 2].rearrange("p (dc o) -> p dc o", o=128),
                    wqkvT.rearrange("(dc p) o -> p dc o", p=128)[:, :, 2 * D + oc * 128:2 * D + (oc + 1) * 128]),
                    "lw", 16)
            G(p2vwl)

            def p2vmm(t, oc=oc, idx=idx):
                W(t, "lw")
                if bank_rel[idx % 2] is not None:
                    W(t, "vd", bank_rel[idx % 2])
                grp = pg[idx % 2]
                for dc in range(NDC):
                    for j in range(4):
                        t.matmul(grp[j], wqk_buf[idx % 2][:, dc * 128:(dc + 1) * 128],
                                 xn_sb[:, dc * 2048 + j * 512:dc * 2048 + (j + 1) * 512],
                                 start=(dc == 0), stop=False)
                last = None
                for j in range(4):
                    last = t.matmul(grp[j], bq2[:, 2 * D + oc * 128:2 * D + (oc + 1) * 128],
                                    ones2[:, :], start=False, stop=True)
                wqk_rel[idx % 2] = inc(last, "mm")
            T(p2vmm)

            def p2vev(v, idx=idx):
                W(v, "mm")
                grp = pg[idx % 2]
                last = None
                for j in range(4):
                    last = v.tensor_copy(vfm_st[:, j * 512:(j + 1) * 512], grp[j])
                bank_rel[idx % 2] = inc(last, "vd")
            V(p2vev)

            def p2vtr(t, idx=idx):
                W(t, "vd")
                if bank_rel[(idx + 1) % 2] is not None:
                    W(t, "vd", bank_rel[(idx + 1) % 2])
                grp = pg[(idx + 1) % 2]
                last = None
                for tg in range(4):
                    for tc in range(4):
                        kc = tg * 4 + tc
                        last = t.matmul(grp[tg][:, tc * 128:(tc + 1) * 128],
                                        vfm_st[:, kc * 128:(kc + 1) * 128], ident[:, :],
                                        start=True, stop=True)
                inc(last, "mm")
            T(p2vtr)

            def p2vte(v, idx=idx):
                W(v, "mm")
                W(v, "st")
                grp = pg[(idx + 1) % 2]
                last = None
                for tg in range(4):
                    last = v.tensor_copy(vtm_st[:, tg * 512:(tg + 1) * 512], grp[tg])
                bank_rel[(idx + 1) % 2] = inc(last, "vd")
            V(p2vte)

            def p2vst(y, oc=oc):
                W(y, "vd")
                inc(y.dma_start(
                    v_sp.rearrange("(tc p) o -> p tc o", p=128)[:, :, oc * 128:(oc + 1) * 128],
                    vtm_st.rearrange("p (tc o) -> p tc o", o=128)), "st", 16)
            Sy(p2vst)

        barrier()

        reg_rel = [None, None]
        treg_rel = [None, None]
        ctx_rel = [None]
        kv_rel = [None, None]
        vv_rel = [None, None]
        ali_rel = [None, None]
        pT_rel = [None, None]
        qtb_rel = [None]
        ctxtb_rel = [None]
        sreg = [0]

        itemc = [0]
        pend = [None]

        def emit_tpv(rec):
            h = rec["h"]; nkc = rec["nkc"]; ip = rec["ip"]
            pairi = rec["pairi"]; vcol = rec["vcol"]; hidx = rec["hidx"]
            qt = rec["qt"]; norm_tok = rec["norm_tok"]
            ngrp = (nkc + 3) // 4
            ev_toks = []
            for gidx in range(ngrp):
                kcs = list(range(gidx * 4, min((gidx + 1) * 4, nkc)))
                tsel = gidx % 2
                treg = psD if tsel == 0 else psE

                def p3tr(t, kcs=kcs, treg=treg, tsel=tsel, ip=ip, norm_tok=norm_tok):
                    W(t, "vd", norm_tok)
                    if treg_rel[tsel] is not None:
                        W(t, "vd", treg_rel[tsel])
                    last = None
                    for ii, kc in enumerate(kcs):
                        last = t.matmul(treg[:, ii * 128:(ii + 1) * 128],
                                        p_ib[ip][:, kc * 128:(kc + 1) * 128], ident[:, :],
                                        start=True, stop=True)
                    inc(last, "mm")
                T(p3tr)

                def p3te(v, kcs=kcs, treg=treg, tsel=tsel, ip=ip):
                    W(v, "mm")
                    treg_rel[tsel] = inc(
                        v.tensor_copy(pT_ib[ip][:, kcs[0] * 128:(kcs[-1] + 1) * 128],
                                      treg[:, 0:len(kcs) * 128]), "vd")
                    ev_toks.append(treg_rel[tsel])
                V(p3te)

            def p3pv(t, qt=qt, nkc=nkc, vcol=vcol, hidx=hidx, pairi=pairi, ip=ip):
                W(t, "vd", ev_toks[-1])
                if qt == 0 and ctx_rel[0] is not None:
                    W(t, "vd", ctx_rel[0])
                last = None
                for kc in range(nkc):
                    last = t.matmul(psC[:, qt * 128:(qt + 1) * 128],
                                    v_buf[pairi % 2][:, kc * 256 + vcol:kc * 256 + vcol + 128],
                                    pT_ib[ip][:, kc * 128:(kc + 1) * 128],
                                    start=(kc == 0), stop=(kc == nkc - 1))
                tok = inc(last, "mm")
                kv_rel[hidx % 2] = tok
                if h % 2 == 1 or h == H - 1:
                    vv_rel[pairi % 2] = tok
                ali_rel[hidx % 2] = tok
                pT_rel[ip] = tok
            T(p3pv)

            if rec["qt"] == 3:
                def p3cx(v, h=h):
                    W(v, "mm")
                    if ctxtb_rel[0] is not None:
                        W(v, "mm", ctxtb_rel[0])
                    ctx_rel[0] = inc(
                        v.tensor_copy(ctx_tb[:, h * 512:(h + 1) * 512], psC[:, :]), "vd")
                V(p3cx)

        for tb in range(NTB):
            krmax = (tb + 1) * TB

            def p3q(y, tb=tb):
                if qtb_rel[0] is not None:
                    W(y, "mm", qtb_rel[0])
                inc(y.dma_start(
                    q_tb.rearrange("p (h t) -> p h t", t=TB),
                    q_sp.rearrange("(h p) t -> p h t", p=128)[:, :, tb * TB:(tb + 1) * TB]),
                    "lx", 16)
            Sy(p3q)

            first_item = [True]
            for h in range(H):
                hidx = tb * H + h
                pairi = (tb * (H // 2) + h // 2)
                vcol = (h % 2) * 128

                def p3kv(y, h=h, tb=tb, krmax=krmax, hidx=hidx, pairi=pairi):
                    if kv_rel[hidx % 2] is not None:
                        W(y, "mm", kv_rel[hidx % 2])
                    inc(y.dma_start(k_buf[hidx % 2][:, 0:krmax],
                                    k_sp[h * 128:(h + 1) * 128, 0:krmax]), "lx", 16)
                    if h % 2 == 0:
                        if vv_rel[pairi % 2] is not None:
                            W(y, "mm", vv_rel[pairi % 2])
                        inc(y.dma_start(
                            v_buf[pairi % 2][:, 0:(krmax // 128) * 256].rearrange(
                                "p (kc o) -> p kc o", o=256),
                            v_sp.rearrange("(kc p) o -> p kc o", p=128)[:, 0:krmax // 128,
                                                                        h * 128:(h + 2) * 128]),
                            "lx", 16)
                Sy(p3kv)

                def p3al(g, h=h, hidx=hidx):
                    if ali_rel[hidx % 2] is not None:
                        W(g, "mm", ali_rel[hidx % 2])
                    inc(g.dma_start(ali_buf[hidx % 2], ali2d[:, h * S:(h + 1) * S]), "lw", 16)
                G(p3al)

                for qt in range(4):
                    qtg = tb * 4 + qt
                    kr = (qtg + 1) * 128
                    nkc = kr // 128
                    chunks = [(0, min(kr, 1024))]
                    if kr > 1024:
                        chunks.append((1024, kr - 1024))
                    nch = len(chunks)
                    ip = itemc[0] % 2
                    itemc[0] += 1
                    regs_used = []

                    max_toks = []
                    for ci, (c0, cw) in enumerate(chunks):
                        reg = sreg[0] % 2
                        sreg[0] += 1
                        r0 = reg * 1024
                        regs_used.append((reg, r0, c0, cw, ci))

                        def p3sc(t, h=h, qt=qt, c0=c0, cw=cw, r0=r0, reg=reg, hidx=hidx, ip=ip):
                            W(t, "lx")
                            W(t, "lw")
                            if first_item[0]:
                                first_item[0] = False
                                W(t, "vd")
                            if reg_rel[reg] is not None:
                                W(t, "sd", reg_rel[reg])
                            for j in range(0, cw, 512):
                                w = min(512, cw - j)
                                t.matmul(psA[:, r0 + j:r0 + j + w],
                                         q_tb[:, h * TB + qt * 128:h * TB + (qt + 1) * 128],
                                         k_buf[hidx % 2][:, c0 + j:c0 + j + w],
                                         start=True, stop=False)
                            last = None
                            for j in range(0, cw, 512):
                                w = min(512, cw - j)
                                last = t.matmul(psA[:, r0 + j:r0 + j + w], ones2[:, 0:128],
                                                ali_buf[hidx % 2][:, c0 + j:c0 + j + w],
                                                start=False, stop=True)
                            inc(last, "mm")
                        T(p3sc)

                        is_last = (c0 + cw == kr)

                        def p3sm(v, r0=r0, cw=cw, is_last=is_last, ci=ci):
                            W(v, "mm")
                            if is_last:
                                inc(v.tensor_tensor(psA[:, r0 + cw - 128:r0 + cw],
                                                    psA[:, r0 + cw - 128:r0 + cw], tri[:, :],
                                                    ALU.add), "vd")
                                W(v, "vd")
                            max_toks.append(
                                inc(v.reduce_max(negmax[:, ci:ci + 1], psA[:, r0:r0 + cw],
                                                 axis=AXX, negate=True), "vd"))
                        V(p3sm)

                    if nch == 2:
                        def p3mc(v):
                            W(v, "vd", max_toks[-1])
                            inc(v.tensor_tensor(negmax[:, 0:1], negmax[:, 0:1], negmax[:, 1:2],
                                                ALU.min), "vd")
                        V(p3mc)
                    cmb_tok = Cn["vd"]

                    def p3ex(s, ip=ip, cmb_tok=cmb_tok):
                        W(s, "vd", cmb_tok)
                        if pT_rel[ip] is not None:
                            W(s, "mm", pT_rel[ip])
                        last = None
                        for (reg, r0, c0, cw, ci) in regs_used:
                            last = s.activation(p_ib[ip][:, c0:c0 + cw], psA[:, r0:r0 + cw],
                                                AF.Exp, bias=negmax[:, 0:1], scale=1.0,
                                                accum_out=rowsum[:, ci:ci + 1])
                            reg_rel[reg] = inc(last, "sd")
                    Sc(p3ex)

                    def p3no(v, nch=nch, kr=kr, ip=ip):
                        W(v, "sd")
                        if nch == 2:
                            inc(v.tensor_tensor(rowsum[:, 0:1], rowsum[:, 0:1], rowsum[:, 1:2],
                                                ALU.add), "vd")
                            W(v, "vd")
                        inc(v.reciprocal(rcs[:, 0:1], rowsum[:, 0:1]), "vd")
                        W(v, "vd")
                        inc(v.tensor_scalar(p_ib[ip][:, 0:kr], p_ib[ip][:, 0:kr], rcs[:, 0:1],
                                            None, ALU.mult), "vd")
                    V(p3no)
                    norm_tok = Cn["vd"]

                    if pend[0] is not None:
                        emit_tpv(pend[0])
                    pend[0] = dict(h=h, qt=qt, kr=kr, nkc=nkc, hidx=hidx, pairi=pairi,
                                   vcol=vcol, ip=ip, norm_tok=norm_tok)

            if pend[0] is not None:
                emit_tpv(pend[0])
                pend[0] = None
            qtb_rel[0] = Cn["mm"]

            wo_rel = [None, None]
            sst_rel = [None, None]
            for pr in range(2):
                for dc in range(NDC):
                    wi = pr * NDC + dc

                    def p3wl(g, dc=dc, wi=wi):
                        if wo_rel[wi % 2] is not None:
                            W(g, "mm", wo_rel[wi % 2])
                        inc(g.dma_start(wo_buf[wi % 2], woT[dc * 128:(dc + 1) * 128, :]),
                            "lw", 16)
                    G(p3wl)

                    def p3wm(t, pr=pr, dc=dc, wi=wi):
                        W(t, "lw")
                        if dc == 0:
                            W(t, "vd")
                        banks = [psA[:, j * 512:(j + 1) * 512] for j in range(4)] + \
                                [psB[:, :], psC[:, :], psD[:, :], psE[:, :]]
                        last = None
                        for tci in range(2):
                            tc = pr * 2 + tci
                            for j in range(4):
                                last = t.matmul(
                                    banks[tci * 4 + j],
                                    ctx_tb[:, dc * 512 + tc * 128:dc * 512 + (tc + 1) * 128],
                                    wo_buf[wi % 2][:, j * 512:(j + 1) * 512],
                                    start=(dc == 0), stop=False)
                        wo_rel[wi % 2] = inc(last, "mm")
                    T(p3wm)

                def p3wb(t, pr=pr):
                    banks = [psA[:, j * 512:(j + 1) * 512] for j in range(4)] + \
                            [psB[:, :], psC[:, :], psD[:, :], psE[:, :]]
                    last = None
                    for tci in range(2):
                        for j in range(4):
                            last = t.matmul(banks[tci * 4 + j], ones2[:, 0:128],
                                            bo2[:, j * 512:(j + 1) * 512],
                                            start=False, stop=True)
                    inc(last, "mm")
                T(p3wb)

                def p3we(v, pr=pr):
                    W(v, "mm")
                    if sst_rel[0] is not None:
                        W(v, "st")
                    banks = [psA[:, j * 512:(j + 1) * 512] for j in range(4)] + \
                            [psB[:, :], psC[:, :], psD[:, :], psE[:, :]]
                    last = None
                    for tci in range(2):
                        for j in range(4):
                            last = v.tensor_copy(s_st[tci][:, j * 512:(j + 1) * 512],
                                                 banks[tci * 4 + j])
                    inc(last, "vd")
                V(p3we)

                def p3ws(y, pr=pr, tb=tb):
                    W(y, "vd")
                    for tci in range(2):
                        tc = pr * 2 + tci
                        inc(y.dma_start(s_sp[tb * TB + tc * 128:tb * TB + (tc + 1) * 128, :],
                                        s_st[tci]), "st", 16)
                    sst_rel[0] = Cn["st"]
                Sy(p3ws)
            ctxtb_rel[0] = Cn["mm"]
            ctx_rel[0] = Cn["vd"]
            treg_rel[0] = Cn["vd"]
            treg_rel[1] = Cn["vd"]

        barrier()

        wfc_rel = [None, None]
        wp_rel = [None]
        acc_sil = None
        for th in range(NTH):
            def p4xn(y, th=th):
                if acc_sil is not None:
                    W(y, "sd", acc_sil)
                inc(y.dma_start(
                    XN4[:, :].rearrange("p (dc t) -> p dc t", t=TH),
                    xn_sp.rearrange("(dc p) t -> p dc t", p=128)[:, :, th * TH:(th + 1) * TH]),
                    "lx", 16)
            Sy(p4xn)

            fcn = [0]
            fcb_rel = [[None, None], [None, None]]
            def p4g1(v, th=th):
                W(v, "mm")
                W(v, "st")
                dsl = d_row[:, th * TH:(th + 1) * TH]
                inc(v.tensor_scalar(abs_row, dsl, -1.0, None, ALU.mult), "vd")
                W(v, "vd")
                inc(v.tensor_tensor(abs_row, abs_row, dsl, ALU.max), "vd")
            V(p4g1)

            def p4g2(s):
                W(s, "vd")
                W(s, "st")
                inc(s.activation(grow[1], abs_row, AF.Sigmoid), "sd")
            Sc(p4g2)

            def p4g3(v, th=th):
                W(v, "sd")
                dsl = d_row[:, th * TH:(th + 1) * TH]
                inc(v.tensor_scalar(grow[0], dsl, 0.0, None, ALU.is_ge), "vd")
                W(v, "vd")
                inc(v.tensor_tensor(grow[0], grow[0], grow[1], ALU.mult), "vd")
                W(v, "vd")
                inc(v.tensor_tensor(grow[1], grow[1], grow[0], ALU.subtract), "vd")
            V(p4g3)

            for j in range(2):
                def p4gb(t, j=j):
                    W(t, "vd")
                    t.matmul(psD[:, :], onesf1[:, :], grow[0][:, j * 512:(j + 1) * 512],
                             start=True, stop=True)
                    inc(t.matmul(psE[:, :], onesf1[:, :], grow[1][:, j * 512:(j + 1) * 512],
                                 start=True, stop=True), "mm")
                T(p4gb)

                def p4ge(v, j=j):
                    W(v, "mm")
                    v.tensor_copy(g_bc[0][:, j * 512:(j + 1) * 512], psD[:, :])
                    inc(v.tensor_copy(g_bc[1][:, j * 512:(j + 1) * 512], psE[:, :]), "vd")
                V(p4ge)
                fcb_rel[0] = [Cn["vd"], Cn["sd"]]
                fcb_rel[1] = [Cn["vd"], Cn["sd"]]

            accw_rel = {}
            for hb in range(NHB):
                for e in range(NE):
                    for pc in range(HB // WPC):
                        widx = (th * NHB + hb) * (NE * 2) + e * 2 + pc

                        def p4fw(g, e=e, hb=hb, pc=pc, widx=widx):
                            if wfc_rel[widx % 2] is not None:
                                W(g, "mm", wfc_rel[widx % 2])
                            h0 = hb * HB + pc * WPC
                            inc(g.dma_start(
                                wfc_buf[widx % 2].rearrange("p (dc o) -> p dc o", o=WPC),
                                wfcT[e].rearrange("(dc p) o -> p dc o", p=128)[:, :, h0:h0 + WPC]),
                                "lw", 16)
                        G(p4fw)

                        for hcr in range(WPC // 128):
                            hcl = pc * 2 + hcr
                            hcg = fcn[0]; fcn[0] += 1
                            par = hcg % 2
                            abank = [psB, psC] if par == 0 else [psD, psE]

                            def p4fc(t, widx=widx, hcr=hcr, par=par, abank=abank):
                                W(t, "lw")
                                W(t, "lx")
                                if fcb_rel[par][0] is not None:
                                    W(t, "vd", fcb_rel[par][0])
                                    W(t, "sd", fcb_rel[par][1])
                                last = None
                                for tch in range(2):
                                    for dc in range(NDC):
                                        last = t.matmul(
                                            abank[tch][:, :],
                                            wfc_buf[widx % 2][:, dc * WPC + hcr * 128:dc * WPC + (hcr + 1) * 128],
                                            XN4[:, dc * TH + tch * 512:dc * TH + (tch + 1) * 512],
                                            start=(dc == 0), stop=(dc == NDC - 1))
                                wfc_rel[widx % 2] = inc(last, "mm")
                            T(p4fc)

                            def p4av(v, par=par, abank=abank):
                                W(v, "mm")
                                W(v, "gd")
                                last = None
                                for tch in range(2):
                                    last = v.tensor_scalar(wv_t[par][tch], abank[tch][:, :],
                                                           0.0, 0.25, ALU.is_ge, ALU.max)
                                tok = inc(last, "vd")
                                fcb_rel[par][0] = tok
                            V(p4av)

                            def p4as(s, par=par, abank=abank):
                                W(s, "mm")
                                W(s, "gd")
                                last = None
                                for tch in range(2):
                                    last = s.activation(sqs_t[par][tch], abank[tch][:, :], AF.Square)
                                tok = inc(last, "sd")
                                fcb_rel[par][1] = tok
                            Sc(p4as)

                            def p4ag(g, e=e, hcl=hcl, par=par):
                                W(g, "vd")
                                W(g, "sd")
                                for tch in range(2):
                                    g.tensor_tensor(sqs_t[par][tch], sqs_t[par][tch],
                                                    wv_t[par][tch], ALU.mult)
                                last = None
                                for tch in range(2):
                                    last = g.tensor_tensor(
                                        a_buf[e][:, hcl * 1024 + tch * 512:hcl * 1024 + (tch + 1) * 512],
                                        sqs_t[par][tch], g_bc[e][:, tch * 512:(tch + 1) * 512],
                                        ALU.mult)
                                inc(last, "gd")
                            G(p4ag)

                for e in range(NE):
                    def p4pw(g, e=e, hb=hb):
                        if wp_rel[0] is not None:
                            W(g, "mm", wp_rel[0])
                        inc(g.dma_start(
                            wp_buf[e].rearrange("p (hc o) -> p hc o", o=2048),
                            wpT[e].rearrange("(hc p) o -> p hc o", p=128)[:, hb * (HB // 128):(hb + 1) * (HB // 128), :]),
                            "lw", 16)
                    G(p4pw)

                for tc in range(TH // 128):
                    par = tc % 2
                    banks = ([psA[:, j * 512:(j + 1) * 512] for j in range(4)] if par == 0
                             else [psB[:, :], psC[:, :], psD[:, :], psE[:, :]])

                    def p4pj(t, tc=tc, hb=hb, par=par, banks=banks):
                        W(t, "lw")
                        W(t, "gd")
                        rel = accw_rel.get(par)
                        if rel is not None:
                            W(t, "vd", rel)
                        last = None
                        for db in range(4):
                            first = True
                            for e in range(NE):
                                for hcl in range(HB // 128):
                                    last = t.matmul(
                                        banks[db],
                                        a_buf[e][:, hcl * 1024 + tc * 128:hcl * 1024 + (tc + 1) * 128],
                                        wp_buf[e][:, hcl * 2048 + db * 512:hcl * 2048 + (db + 1) * 512],
                                        start=first, stop=(e == NE - 1 and hcl == HB // 128 - 1))
                                    first = False
                        inc(last, "mm")
                    T(p4pj)

                    def p4pe(v, tc=tc, hb=hb, par=par, banks=banks):
                        W(v, "mm")
                        if hb == 0:
                            W(v, "sd")
                        dst = acc[:, tc * 2048:(tc + 1) * 2048]
                        if par == 0:
                            src = psA[:, 0:2048]
                            if hb == 0:
                                last = v.tensor_copy(dst, src)
                            else:
                                last = v.tensor_tensor(dst, dst, src, ALU.add)
                        else:
                            last = None
                            for db in range(4):
                                d2 = acc[:, tc * 2048 + db * 512:tc * 2048 + (db + 1) * 512]
                                if hb == 0:
                                    last = v.tensor_copy(d2, banks[db])
                                else:
                                    last = v.tensor_tensor(d2, d2, banks[db], ALU.add)
                        accw_rel[par] = inc(last, "vd")
                    V(p4pe)
                wp_rel[0] = Cn["mm"]
                fcb_rel[0] = [Cn["vd"], Cn["sd"]]
                fcb_rel[1] = [Cn["vd"], Cn["sd"]]

            cb_rel = None
            for tc in range(TH // 128):
                def p4cl(y, tc=tc, th=th):
                    W(y, "vd")
                    W(y, "gd")
                    if cb_rel is not None:
                        W(y, "st", cb_rel)
                    r0 = th * TH + tc * 128
                    inc(y.dma_start(x_cb, xtm[r0:r0 + 128, :]), "lx", 16)
                    inc(y.dma_start(s_cb, s_sp[r0:r0 + 128, :]), "lx", 16)
                Sy(p4cl)

                def p4cs(s, tc=tc):
                    W(s, "lx")
                    W(s, "vd")
                    W(s, "gd")
                    W(s, "st")
                    inc(s.activation(o_cb, acc[:, tc * 2048:(tc + 1) * 2048], AF.Silu), "sd")
                Sc(p4cs)

                def p4cv(v, tc=tc):
                    W(v, "sd")
                    inc(v.tensor_tensor(o_cb, s_cb, o_cb, ALU.mult), "vd")
                    W(v, "vd")
                    inc(v.tensor_tensor(o_cb, o_cb, x_cb, ALU.add), "vd")
                V(p4cv)

                def p4cst(y, tc=tc, th=th):
                    W(y, "vd")
                    r0 = th * TH + tc * 128
                    inc(y.dma_start(out[r0:r0 + 128, :], o_cb), "st", 16)
                Sy(p4cst)
                cb_rel = Cn["st"]
            acc_sil = Cn["sd"]

        def fin(y):
            W(y, "st")
        Sy(fin)

    return nc


def _stage_inputs(L0_attn_x, L0_attn_Wqkv, L0_attn_b_qkv, L0_attn_Wout,
                  L0_attn_b_out, L0_ffn_w_router, L0_ffn_w_fc, L0_ffn_w_proj):
    x = np.asarray(L0_attn_x, dtype=np.float32)
    wqkvT = np.ascontiguousarray(np.asarray(L0_attn_Wqkv, np.float32).T)
    bqkv = np.ascontiguousarray(np.asarray(L0_attn_b_qkv, np.float32)[None, :])
    woT = np.ascontiguousarray(np.asarray(L0_attn_Wout, np.float32).T)
    boutv = np.ascontiguousarray(np.asarray(L0_attn_b_out, np.float32)[None, :])
    wrT = np.ascontiguousarray(np.asarray(L0_ffn_w_router, np.float32).T)
    wfcT = np.ascontiguousarray(np.asarray(L0_ffn_w_fc, np.float32).transpose(0, 2, 1))
    wpT = np.ascontiguousarray(np.asarray(L0_ffn_w_proj, np.float32).transpose(0, 2, 1))

    identd = np.eye(128, dtype=np.float32)
    trid = np.where(np.arange(128)[:, None] >= np.arange(128)[None, :], 0.0, NEG
                    ).astype(np.float32)
    mvec = np.arange(1, H + 1, dtype=np.float64) * (8.0 / H)
    slopes = 1.0 / 2.0 ** mvec
    pos = np.arange(S, dtype=np.float64) - (S - 1)
    ali = (slopes[:, None] * pos[None, :]).astype(np.float32)
    ahi = ali.astype(ml_dtypes.bfloat16).astype(np.float32)
    alo = (ali - ahi).astype(ml_dtypes.bfloat16).astype(np.float32)
    ali2d = np.ascontiguousarray(
        np.stack([ahi.reshape(-1), alo.reshape(-1)], axis=0))

    shared = dict(wqkvT=wqkvT, bqkv=bqkv, woT=woT, bout=boutv, wrT=wrT,
                  wfcT=wfcT, wpT=wpT, identd=identd, trid=trid, ali2d=ali2d)
    in_maps = []
    for b in range(8):
        m = dict(shared)
        m["xT"] = np.ascontiguousarray(x[b].T)
        m["xtm"] = np.ascontiguousarray(x[b])
        in_maps.append(m)
    return in_maps


def kernel(L0_attn_x, L0_attn_Wqkv, L0_attn_b_qkv, L0_attn_Wout, L0_attn_b_out,
           L0_ffn_w_router, L0_ffn_w_fc, L0_ffn_w_proj):
    in_maps = _stage_inputs(L0_attn_x, L0_attn_Wqkv, L0_attn_b_qkv, L0_attn_Wout,
                            L0_attn_b_out, L0_ffn_w_router, L0_ffn_w_fc,
                            L0_ffn_w_proj)
    nc = build_program()
    res = run_bass_kernel_spmd(nc, in_maps, core_ids=list(range(8)))
    return np.stack([np.asarray(res.results[b]["out"]) for b in range(8)], axis=0)


if __name__ == "__main__":
    build_program()
    print("built ok")
